# version 04
# baseline (speedup 1.0000x reference)
import math
import numpy as np
import ml_dtypes
import concourse.bass as bass
import concourse.mybir as mybir
from concourse.bass_utils import run_bass_kernel_spmd

F32 = mybir.dt.float32
BF16 = mybir.dt.bfloat16
AL = mybir.AluOpType
AF = mybir.ActivationFunctionType
AX = mybir.AxisListType

D = 2048
T = 4096
H = 16
KT = 16
NG = 8
E = 8
ROUTE_T = 2048
EPS = 1e-5
ISQ = 1.0 / math.sqrt(128.0)
EXP_SHIFT = -8.0

BF = np.dtype(ml_dtypes.bfloat16)


class BufSt:
    __slots__ = ("w", "r")

    def __init__(self):
        self.w = None
        self.r = []


class Prog:
    ENGINES = ["tensor", "vector", "scalar", "gpsimd", "sync"]

    def __init__(self):
        self.ops = {e: [] for e in self.ENGINES}
        self.cnt = {}
        self.hw = {e: {} for e in self.ENGINES}

    def em(self, eng, fn, reads=(), writes=(), dma=False):
        deps = []
        for b in reads:
            if b.w is not None:
                deps.append(b.w)
        for b in writes:
            deps.extend(b.r)
            if b.w is not None:
                deps.append(b.w)
        need = {}
        for (s, v) in deps:
            if v > need.get(s, 0):
                need[s] = v
        hw = self.hw[eng]
        waits = []
        for s, v in need.items():
            if v > hw.get(s, 0):
                hw[s] = v
                waits.append((s, v))
        sem = ("q_" + eng) if dma else ("c_" + eng)
        inc = 16 if dma else 1
        self.cnt[sem] = self.cnt.get(sem, 0) + inc
        tok = (sem, self.cnt[sem])
        self.ops[eng].append((waits, fn, sem, inc))
        for b in reads:
            b.r.append(tok)
        for b in writes:
            b.w = tok
            b.r = []
        return tok

    def barrier(self):
        cur = dict(self.cnt)
        for e in self.ENGINES:
            hw = self.hw[e]
            waits = []
            for s, v in cur.items():
                if v > hw.get(s, 0):
                    hw[s] = v
                    waits.append((s, v))
            if waits:
                self.ops[e].append((waits, None, None, 0))


def make_consts():
    c = {}
    c["identity"] = np.eye(128, dtype=np.float32).astype(BF)
    c["ones_col_f"] = np.ones((128, 1), dtype=np.float32)
    c["ones_col_b"] = np.ones((128, 1), dtype=np.float32).astype(BF)
    c["ones_row_f"] = np.ones((1, 128), dtype=np.float32)
    c["one11"] = np.ones((1, 1), dtype=np.float32)
    start = 2.0 ** (-2.0 ** (-(math.log2(H) - 3)))
    slopes = np.array([start ** (i + 1) for i in range(H)], dtype=np.float64)
    lam = np.exp(-slopes)
    pos = np.arange(128, dtype=np.float64)
    dmaskT = np.zeros((H, 128, 128), dtype=np.float64)
    dq = np.zeros((H, 128, 128), dtype=np.float64)
    dk = np.zeros((H, 128, 128), dtype=np.float64)
    for h in range(H):
        diff = pos[None, :] - pos[:, None]
        dmaskT[h] = np.where(diff >= 0, lam[h] ** np.maximum(diff, 0), 0.0)
        dq[h][:, :] = (lam[h] ** (pos + 1.0))[None, :]
        dk[h][:, :] = (lam[h] ** (127.0 - pos))[:, None]
    c["dmaskT"] = dmaskT.astype(np.float32).astype(BF)
    c["decayq"] = dq.astype(np.float32).astype(BF)
    c["decayk"] = dk.astype(np.float32).astype(BF)
    c["lam_c"] = [float(l) ** 128 for l in lam]
    theta = 1.0 / (10000.0 ** (np.arange(0, 128, 2, dtype=np.float64) / 128.0))
    ang = np.outer(theta, np.arange(T, dtype=np.float64))
    cs_ = np.cos(ang).astype(np.float32)
    sn_ = np.sin(ang).astype(np.float32)
    c["ropecos"] = np.vstack([cs_, cs_]).astype(BF)
    c["ropesin"] = np.vstack([-sn_, sn_]).astype(BF)
    c["swap"] = np.roll(np.eye(128, dtype=np.float32), 64, axis=0).astype(BF)
    stairs = np.zeros((4, 128, 512), dtype=np.float32)
    tkv = np.arange(128)[:, None]
    tqv = np.arange(512)[None, :]
    for j in range(4):
        stairs[j] = (tqv >= j * 128 + tkv).astype(np.float32)
    c["stairs"] = stairs.astype(BF)
    return c


def build_nc():
    nc = bass.Bass("TRN2", target_bir_lowering=False)
    P = Prog()

    xin = nc.dram_tensor("xin", [D, T], F32, kind="ExternalInput").ap()
    wqkvT = nc.dram_tensor("wqkvT", [D, 3 * D], F32, kind="ExternalInput").ap()
    bqkv = nc.dram_tensor("bqkv", [1, D], F32, kind="ExternalInput").ap()
    bqkv_cols = nc.dram_tensor("bqkv_cols", [128, 32], F32, kind="ExternalInput").ap()
    woutT = nc.dram_tensor("woutT", [D, D], F32, kind="ExternalInput").ap()
    bout_cols = nc.dram_tensor("bout_cols", [128, 16], F32, kind="ExternalInput").ap()
    watT = nc.dram_tensor("watT", [D, 3 * D], F32, kind="ExternalInput").ap()
    wprojT = nc.dram_tensor("wprojT", [D, D], F32, kind="ExternalInput").ap()
    sw_route = nc.dram_tensor("sw_route", [E, 128, 32768], F32, kind="ExternalInput").ap()
    swb = nc.dram_tensor("swb", [1, E], F32, kind="ExternalInput").ap()
    w1T = nc.dram_tensor("w1T", [E * 128, 128], F32, kind="ExternalInput").ap()
    w2T = nc.dram_tensor("w2T", [E * 128, 128], F32, kind="ExternalInput").ap()
    b1T_d = nc.dram_tensor("b1T", [128, E], F32, kind="ExternalInput").ap()
    b2m_d = nc.dram_tensor("b2m", [E, 128], F32, kind="ExternalInput").ap()
    w3T = nc.dram_tensor("w3T", [D, D], F32, kind="ExternalInput").ap()
    b3_cols = nc.dram_tensor("b3_cols", [128, 16], F32, kind="ExternalInput").ap()

    c_id = nc.dram_tensor("c_id", [128, 128], BF16, kind="ExternalInput").ap()
    c_ocf = nc.dram_tensor("c_ocf", [128, 1], F32, kind="ExternalInput").ap()
    c_ocb = nc.dram_tensor("c_ocb", [128, 1], BF16, kind="ExternalInput").ap()
    c_orf = nc.dram_tensor("c_orf", [1, 128], F32, kind="ExternalInput").ap()
    c_one11 = nc.dram_tensor("c_one11", [1, 1], F32, kind="ExternalInput").ap()
    c_dmaskT = nc.dram_tensor("c_dmaskT", [H * 128, 128], BF16, kind="ExternalInput").ap()
    c_decayq = nc.dram_tensor("c_decayq", [H * 128, 128], BF16, kind="ExternalInput").ap()
    c_decayk = nc.dram_tensor("c_decayk", [H * 128, 128], BF16, kind="ExternalInput").ap()
    c_cos = nc.dram_tensor("c_cos", [128, T], BF16, kind="ExternalInput").ap()
    c_sin = nc.dram_tensor("c_sin", [128, T], BF16, kind="ExternalInput").ap()
    c_swp = nc.dram_tensor("c_swp", [128, 128], BF16, kind="ExternalInput").ap()
    c_stairs = nc.dram_tensor("c_stairs", [4 * 128, 512], BF16, kind="ExternalInput").ap()
    c_negs = nc.dram_tensor("c_negs", [128, 1], F32, kind="ExternalInput").ap()

    out = nc.dram_tensor("out", [D, T], F32, kind="ExternalOutput").ap()
    xs1 = nc.dram_tensor("xs1", [D, T], F32, kind="Internal").ap()
    xs2 = nc.dram_tensor("xs2", [D, T], F32, kind="Internal").ap()
    xs3 = nc.dram_tensor("xs3", [D, T], F32, kind="Internal").ap()
    qkf = nc.dram_tensor("qkf", [2 * D, T], BF16, kind="Internal").ap()
    vtm = nc.dram_tensor("vtm", [T, D], BF16, kind="Internal").ap()
    qkf2 = nc.dram_tensor("qkf2", [2 * D, T], BF16, kind="Internal").ap()
    vtm2 = nc.dram_tensor("vtm2", [T, D], BF16, kind="Internal").ap()

    import contextlib
    stack = contextlib.ExitStack()

    def sb(name, shape, dt):
        return stack.enter_context(nc.sbuf_tensor(name, shape, dt)).ap()

    def pst(name, shape, dt=F32):
        return stack.enter_context(nc.psum_tensor(name, shape, dt)).ap()

    lam_c = make_consts()["lam_c"]

    RES = sb("RES", [128, KT * T], BF16)
    ARF = sb("ARF", [128, 4096], F32)
    ARB = sb("ARB", [128, 16384], BF16)
    ID = sb("ID", [128, 128], BF16)
    SWP = sb("SWP", [128, 128], BF16)
    OCF = sb("OCF", [128, 1], F32)
    OCB = sb("OCB", [128, 1], BF16)
    ORF = sb("ORF", [1, 128], F32)
    ONE11 = sb("ONE11", [1, 1], F32)
    BQC = sb("BQC", [128, 32], F32)
    BOC = sb("BOC", [128, 16], F32)
    B3C = sb("B3C", [128, 16], F32)
    BQROW = sb("BQROW", [1, D], F32)
    SWB = sb("SWB", [1, E], F32)
    B1T = sb("B1T", [128, E], F32)
    B2M = sb("B2M", [8, 128], F32)
    W1B = sb("W1B", [128, E * 128], BF16)
    W2F = sb("W2F", [128, E * 128], F32)
    W2S = sb("W2S", [128, E * 128], BF16)
    MU = sb("MU", [1, 512], F32)
    M2 = sb("M2", [1, 512], F32)
    RS = sb("RS", [1, 512], F32)
    LG = sb("LG", [1, 8], F32)
    PRB = sb("PRB", [1, 8], F32)
    SC1 = sb("SC1", [1, 1], F32)
    SC2 = sb("SC2", [1, 1], F32)
    GR = sb("GR", [1, 8], F32)
    GCOL = sb("GCOL", [8, 1], F32)
    B2S = sb("B2S", [128, 1], F32)
    GB = sb("GB", [128, 8], F32)
    RACC = sb("RACC", [128, 8], F32)
    RCA = sb("RCA", [128, 1], F32)
    RCB = sb("RCB", [128, 1], F32)
    RRC = sb("RRC", [1, 512], F32)
    NEGS = sb("NEGS", [128, 1], F32)

    RESst = [[BufSt() for _ in range(NG)] for _ in range(KT)]
    st_id = BufSt(); st_small = BufSt()
    st_w1 = BufSt(); st_w2f = BufSt(); st_w2s = BufSt()
    MUst = BufSt(); M2st = BufSt(); RSst = BufSt()
    LGst = BufSt(); PRBst = BufSt(); SC1st = BufSt(); SC2st = BufSt()
    GRst = BufSt(); GCOLst = BufSt(); B2Sst = BufSt(); GBst = BufSt()
    RACCst = BufSt(); RCAst = BufSt(); RCBst = BufSt(); RRCst = BufSt()

    PA = pst("PA", [128, 512]); PAs = BufSt()
    PB = pst("PB", [128, 512]); PBs = BufSt()
    PC = pst("PC", [128, 512]); PCs = BufSt()
    PD = pst("PD", [128, 512]); PDs = BufSt()
    PS = pst("PS", [128, 128]); PSs = BufSt()
    PT = pst("PT", [128, 128], BF16); PTs = BufSt()
    PR1 = pst("PR1", [1, 512]); PR1s = BufSt()
    PR2 = pst("PR2", [1, 512]); PR2s = BufSt()

    def dma(eng, dst, src, rd=(), wr=(), cast=False):
        q = "gpsimd" if cast else eng
        return P.em(q, (lambda d=dst, s=src, qq=q: getattr(nc, qq).dma_start(out=d, in_=s)),
                    reads=rd, writes=wr, dma=True)

    dma("sync", ID[:, :], c_id, wr=[st_id])
    dma("sync", SWP[:, :], c_swp, wr=[st_id])
    dma("sync", OCF[:, :], c_ocf, wr=[st_small])
    dma("sync", OCB[:, :], c_ocb, wr=[st_small])
    dma("sync", ORF[:, :], c_orf, wr=[st_small])
    dma("sync", ONE11[:, :], c_one11, wr=[st_small])
    dma("sync", BQC[:, :], bqkv_cols, wr=[st_small])
    dma("sync", BOC[:, :], bout_cols, wr=[st_small])
    dma("sync", B3C[:, :], b3_cols, wr=[st_small])
    dma("sync", BQROW[:, :], bqkv, wr=[st_small])
    dma("sync", SWB[:, :], swb, wr=[st_small])
    dma("sync", B1T[:, :], b1T_d, wr=[st_small])
    dma("sync", B2M[:, :], b2m_d, wr=[st_small])
    dma("sync", NEGS[:, :], c_negs, wr=[st_small])
    dma("sync", W1B.rearrange("p (e m) -> p e m", m=128),
        w1T.rearrange("(e p) m -> p e m", p=128), wr=[st_w1], cast=True)
    dma("sync", W2F.rearrange("p (e m) -> p e m", m=128),
        w2T.rearrange("(e p) m -> p e m", p=128), wr=[st_w2f])
    P.barrier()

    def mm(outap, lhsT, rhs, start, stop, rd, wrst):
        return P.em("tensor",
                    (lambda o=outap, l=lhsT, r=rhs, s0=start, s1=stop:
                     nc.tensor.matmul(o, l, r, start=s0, stop=s1)),
                    reads=rd, writes=wrst)

    def act(outap, inap, func, rd, wrst, bias=None, scale=None):
        kw = {}
        if bias is not None:
            kw["bias"] = bias
        if scale is not None:
            kw["scale"] = scale
        return P.em("scalar",
                    (lambda o=outap, i=inap, f=func, k=dict(kw):
                     nc.scalar.activation(o, i, f, **k)),
                    reads=rd, writes=wrst)

    def vtt(outap, in0, in1, op, rd, wrst):
        return P.em("vector",
                    (lambda o=outap, a=in0, b=in1, p=op:
                     nc.vector.tensor_tensor(o, a, b, op=p)),
                    reads=rd, writes=wrst)

    def vts(outap, in0, s1, op, rd, wrst):
        return P.em("vector",
                    (lambda o=outap, a=in0, s=s1, p=op:
                     nc.vector.tensor_scalar(o, a, s, None, p)),
                    reads=rd, writes=wrst)

    def vred(outap, inap, op, rd, wrst):
        return P.em("vector",
                    (lambda o=outap, i=inap, p=op:
                     nc.vector.tensor_reduce(o, i, AX.X, p)),
                    reads=rd, writes=wrst)

    XT = [ARF[:, 0:512], ARF[:, 512:1024]]
    XTst = [BufSt(), BufSt()]
    XB_ = ARB[:, 0:512]; XBst = BufSt()
    SQ_ = ARB[:, 512:1024]; SQst = BufSt()

    def ln_pass(xsrc):
        for tg in range(NG):
            for k in range(KT):
                i = k % 2
                dma("sync", XT[i], xsrc[k * 128:(k + 1) * 128, tg * 512:(tg + 1) * 512],
                    wr=[XTst[i]])
                act(XB_, XT[i], AF.Copy, rd=[XTst[i]], wrst=[XBst])
                act(SQ_, XT[i], AF.Square, rd=[XTst[i]], wrst=[SQst])
                mm(PR1[0:1, :], OCB[:, :], XB_, k == 0, k == KT - 1,
                   rd=[XBst, st_small], wrst=[PR1s])
                mm(PR2[0:1, :], OCB[:, :], SQ_, k == 0, k == KT - 1,
                   rd=[SQst, st_small], wrst=[PR2s])
            act(MU[0:1, :], PR1[0:1, :], AF.Copy, rd=[PR1s], wrst=[MUst], scale=1.0 / D)
            act(M2[0:1, :], PR2[0:1, :], AF.Copy, rd=[PR2s], wrst=[M2st], scale=1.0 / D)
            vtt(RS[0:1, :], MU[0:1, :], MU[0:1, :], AL.mult, rd=[MUst], wrst=[RSst])
            vtt(M2[0:1, :], M2[0:1, :], RS[0:1, :], AL.subtract, rd=[M2st, RSst], wrst=[M2st])
            vts(M2[0:1, :], M2[0:1, :], EPS, AL.add, rd=[M2st], wrst=[M2st])
            act(RS[0:1, :], M2[0:1, :], AF.Sqrt, rd=[M2st], wrst=[RSst])
            P.em("vector", (lambda o=RS[0:1, :], i=RS[0:1, :]: nc.vector.reciprocal(o, i)),
                 reads=[RSst], writes=[RSst])
            mm(PD[:, :], ORF[0:1, :], MU[0:1, :], True, True, rd=[MUst, st_small], wrst=[PDs])
            mm(PC[:, :], ORF[0:1, :], RS[0:1, :], True, True, rd=[RSst, st_small], wrst=[PCs])
            for k in range(KT):
                i = k % 2
                dma("sync", XT[i], xsrc[k * 128:(k + 1) * 128, tg * 512:(tg + 1) * 512],
                    wr=[XTst[i]])
                vtt(XT[i], XT[i], PD[:, :], AL.subtract, rd=[XTst[i], PDs], wrst=[XTst[i]])
                vtt(RES[:, k * T + tg * 512: k * T + tg * 512 + 512], XT[i], PC[:, :],
                    AL.mult, rd=[XTst[i], PCs], wrst=[RESst[k][tg]])

    WT = [ARB[:, 1024:3072], ARB[:, 3072:5120]]
    WTst = [BufSt(), BufSt()]

    def fm_proj(wsrc, jlist, epi):
        for idx, j in enumerate(jlist):
            i = idx % 2
            dma("sync", WT[i].rearrange("p (k m) -> p k m", m=128),
                wsrc[:, j * 128:(j + 1) * 128].rearrange("(k p) m -> p k m", p=128),
                wr=[WTst[i]], cast=True)
            for tg in range(NG):
                pp, pps = (PA, PAs) if tg % 2 == 0 else (PB, PBs)
                for k in range(KT):
                    mm(pp[:, :], WT[i][:, k * 128:(k + 1) * 128],
                       RES[:, k * T + tg * 512: k * T + tg * 512 + 512],
                       k == 0, k == KT - 1,
                       rd=[WTst[i], RESst[k][tg]], wrst=[pps])
                epi(j, tg, pp, pps)

    def tm_proj(wsrc, dst, vbias):
        WVG = ARB[:, 1024:9216]
        WVGst = BufSt()
        OT = [ARB[:, 9216:9728], ARB[:, 9728:10240]]
        OTst = [BufSt(), BufSt()]
        VB = ARF[:, 1024:1536]; VBst = BufSt()
        for g in range(4):
            dma("sync", WVG.rearrange("p (k m) -> p k m", m=512),
                wsrc[:, g * 512:(g + 1) * 512].rearrange("(k p) m -> p k m", p=128),
                wr=[WVGst], cast=True)
            if vbias is not None:
                mm(PD[:, :], ORF[0:1, :], vbias[0:1, g * 512:(g + 1) * 512], True, True,
                   rd=[st_small], wrst=[PDs])
                act(VB, PD[:, :], AF.Copy, rd=[PDs], wrst=[VBst])
            for c in range(T // 128):
                pp, pps = (PA, PAs) if c % 2 == 0 else (PB, PBs)
                tgc, off = c // 4, (c % 4) * 128
                for k in range(KT):
                    mm(pp[:, :], RES[:, k * T + tgc * 512 + off: k * T + tgc * 512 + off + 128],
                       WVG[:, k * 512:(k + 1) * 512], k == 0, k == KT - 1,
                       rd=[WVGst, RESst[k][tgc]], wrst=[pps])
                oi = c % 2
                if vbias is not None:
                    vtt(OT[oi], pp[:, :], VB, AL.add, rd=[pps, VBst], wrst=[OTst[oi]])
                else:
                    act(OT[oi], pp[:, :], AF.Copy, rd=[pps], wrst=[OTst[oi]])
                dma("scalar", dst[c * 128:(c + 1) * 128, g * 512:(g + 1) * 512],
                    OT[oi], rd=[OTst[oi]])

    XR = [ARF[:, 1024:1536], ARF[:, 1536:2048]]
    XRst = [BufSt(), BufSt()]
    TA = [ARF[:, 2048:2560], ARF[:, 2560:3072]]
    TAst = [BufSt(), BufSt()]

    def make_proj_epi(xres, xdst, bias_cols):
        def epi(j, tg, pp, pps):
            i = tg % 2
            dma("sync", XR[i], xres[j * 128:(j + 1) * 128, tg * 512:(tg + 1) * 512],
                wr=[XRst[i]])
            if bias_cols is not None:
                vts(TA[i], pp[:, :], bias_cols[:, j:j + 1], AL.add,
                    rd=[pps, st_small], wrst=[TAst[i]])
            else:
                act(TA[i], pp[:, :], AF.Copy, rd=[pps], wrst=[TAst[i]])
            vtt(TA[i], TA[i], XR[i], AL.add, rd=[TAst[i], XRst[i]], wrst=[TAst[i]])
            dma("scalar", xdst[j * 128:(j + 1) * 128, tg * 512:(tg + 1) * 512],
                TA[i], rd=[TAst[i]])
        return epi

    ln_pass(xin)
    P.barrier()

    QS = [ARB[:, 5120:5632], ARB[:, 5632:6144]]
    QSst = [BufSt(), BufSt()]

    def a1_epi(j, tg, pp, pps):
        i = tg % 2
        vts(QS[i], pp[:, :], BQC[:, j:j + 1], AL.add, rd=[pps, st_small],
            wrst=[QSst[i]])
        dma("scalar", qkf[j * 128:(j + 1) * 128, tg * 512:(tg + 1) * 512],
            QS[i], rd=[QSst[i]])

    fm_proj(wqkvT, list(range(32)), a1_epi)
    P.barrier()
    tm_proj(wqkvT[:, 2 * D:3 * D], vtm, BQROW)
    P.barrier()

    QC = ARB[:, 0:2048]; QCst = BufSt()
    KC = ARB[:, 2048:4096]; KCst = BufSt()
    VC = ARB[:, 4096:6144]; VCst = BufSt()
    DMK = ARB[:, 6144:8192]; DMKst = BufSt()
    DCQ = ARB[:, 8192:10240]; DCQst = BufSt()
    DCK = ARB[:, 10240:12288]; DCKst = BufSt()
    SBF = ARB[:, 12288:14336]; SBFst = [BufSt() for _ in range(H)]
    SCB = ARB[:, 14336:14464]; SCBst = BufSt()
    STM = ARB[:, 14464:14592]; STMst = BufSt()
    QD = ARB[:, 14592:14720]; QDst = BufSt()
    KD = ARB[:, 14720:14848]; KDst = BufSt()
    SF = ARF[:, 0:2048]; SFst = [BufSt() for _ in range(H)]

    dma("sync", DMK.rearrange("p (h m) -> p h m", m=128),
        c_dmaskT.rearrange("(h p) m -> p h m", p=128), wr=[DMKst])
    dma("sync", DCQ.rearrange("p (h m) -> p h m", m=128),
        c_decayq.rearrange("(h p) m -> p h m", p=128), wr=[DCQst])
    dma("sync", DCK.rearrange("p (h m) -> p h m", m=128),
        c_decayk.rearrange("(h p) m -> p h m", p=128), wr=[DCKst])
    P.em("vector", (lambda o=SF: nc.vector.memset(o, 0.0)), writes=SFst)
    P.em("vector", (lambda o=SBF: nc.vector.memset(o, 0.0)), writes=SBFst)

    for c in range(T // 128):
        dma("sync", QC.rearrange("p (h m) -> p h m", m=128),
            qkf[0:D, c * 128:(c + 1) * 128].rearrange("(h p) m -> p h m", p=128),
            wr=[QCst])
        dma("sync", KC.rearrange("p (h m) -> p h m", m=128),
            qkf[D:2 * D, c * 128:(c + 1) * 128].rearrange("(h p) m -> p h m", p=128),
            wr=[KCst])
        dma("sync", VC, vtm[c * 128:(c + 1) * 128, :], wr=[VCst])
        for h in range(H):
            hs = slice(h * 128, (h + 1) * 128)
            mm(PS[:, :], KC[:, hs], QC[:, hs], True, True, rd=[KCst, QCst], wrst=[PSs])
            act(SCB, PS[:, :], AF.Copy, rd=[PSs], wrst=[SCBst])
            vtt(STM, SCB, DMK[:, hs], AL.mult, rd=[SCBst, DMKst], wrst=[STMst])
            mm(PC[:, 0:128], VC[:, hs], STM, True, False, rd=[VCst, STMst], wrst=[PCs])
            vtt(QD, QC[:, hs], DCQ[:, hs], AL.mult, rd=[QCst, DCQst], wrst=[QDst])
            mm(PC[:, 0:128], SBF[:, hs], QD, False, True, rd=[SBFst[h], QDst], wrst=[PCs])
            P.em("tensor", (lambda o=PT[:, :], a=KC[:, hs], b=ID[:, :]:
                            nc.tensor.transpose(o, a, b)),
                 reads=[KCst, st_id], writes=[PTs])
            vtt(KD, PT[:, :], DCK[:, hs], AL.mult, rd=[PTs, DCKst], wrst=[KDst])
            mm(PD[:, 0:128], KD, VC[:, hs], True, True, rd=[KDst, VCst], wrst=[PDs])
            vts(SF[:, hs], SF[:, hs], lam_c[h], AL.mult, rd=[SFst[h]], wrst=[SFst[h]])
            vtt(SF[:, hs], PD[:, 0:128], SF[:, hs], AL.add, rd=[PDs, SFst[h]], wrst=[SFst[h]])
            act(SBF[:, hs], SF[:, hs], AF.Copy, rd=[SFst[h]], wrst=[SBFst[h]])
            act(RES[:, h * T + c * 128: h * T + c * 128 + 128], PC[:, 0:128], AF.Copy,
                rd=[PCs], wrst=[RESst[h][c // 4]])
    P.barrier()

    fm_proj(woutT, list(range(16)), make_proj_epi(xin, xs1, BOC))
    P.barrier()

    def ffn_stage(xsrc, xdst):
        ln_pass(xsrc)
        P.barrier()
        SWS = [ARB[:, 2048:4096], ARB[:, 4096:6144]]
        SWSst = [BufSt(), BufSt()]
        DUMP = ARB[:, 0:2048]; DUMPst = BufSt()
        nseg = 16
        for e in range(E):
            for s in range(nseg):
                i = (e * nseg + s) % 2
                dma("sync", SWS[i], sw_route[e, :, s * 2048:(s + 1) * 2048],
                    wr=[SWSst[i]], cast=True)
                rin = RES[:, s * T: s * T + ROUTE_T]
                rdl = [SWSst[i]] + res_cells_route(s)
                vtt(DUMP, rin, SWS[i], AL.mult, rd=rdl, wrst=[DUMPst])
                if s == 0:
                    vred(RCA[:, :], DUMP, AL.add, rd=[DUMPst], wrst=[RCAst])
                else:
                    vred(RCB[:, :], DUMP, AL.add, rd=[DUMPst], wrst=[RCBst])
                    vtt(RCA[:, :], RCA[:, :], RCB[:, :], AL.add,
                        rd=[RCAst, RCBst], wrst=[RCAst])
            act(RACC[:, e:e + 1], RCA[:, :], AF.Copy, rd=[RCAst], wrst=[RACCst])
        mm(PS[0:1, 0:8], OCF[:, :], RACC[:, :], True, True, rd=[RACCst, st_small], wrst=[PSs])
        vtt(LG[0:1, :], PS[0:1, 0:8], SWB[0:1, :], AL.add, rd=[PSs, st_small], wrst=[LGst])
        vred(SC1[0:1, :], LG[0:1, :], AL.max, rd=[LGst], wrst=[SC1st])
        vts(SC1[0:1, :], SC1[0:1, :], -1.0, AL.mult, rd=[SC1st], wrst=[SC1st])
        act(PRB[0:1, :], LG[0:1, :], AF.Exp, rd=[LGst, SC1st], wrst=[PRBst],
            bias=SC1[0:1, 0:1], scale=1.0)
        vred(SC2[0:1, :], PRB[0:1, :], AL.add, rd=[PRBst], wrst=[SC2st])
        P.em("vector", (lambda o=SC2[0:1, :], i=SC2[0:1, :]: nc.vector.reciprocal(o, i)),
             reads=[SC2st], writes=[SC2st])
        vts(PRB[0:1, :], PRB[0:1, :], SC2[0:1, 0:1], AL.mult, rd=[PRBst, SC2st], wrst=[PRBst])
        vred(SC1[0:1, :], PRB[0:1, :], AL.max, rd=[PRBst], wrst=[SC1st])
        vts(GR[0:1, :], PRB[0:1, :], SC1[0:1, 0:1], AL.is_ge, rd=[PRBst, SC1st], wrst=[GRst])
        vtt(GR[0:1, :], GR[0:1, :], PRB[0:1, :], AL.mult, rd=[GRst, PRBst], wrst=[GRst])
        vtt(GR[0:1, :], PRB[0:1, :], GR[0:1, :], AL.subtract, rd=[GRst, PRBst], wrst=[GRst])
        vred(SC2[0:1, :], GR[0:1, :], AL.max, rd=[GRst], wrst=[SC2st])
        vts(GR[0:1, :], PRB[0:1, :], SC2[0:1, 0:1], AL.is_ge, rd=[PRBst, SC2st], wrst=[GRst])
        vtt(GR[0:1, :], GR[0:1, :], PRB[0:1, :], AL.mult, rd=[GRst, PRBst], wrst=[GRst])
        mm(PS[0:8, 0:1], GR[0:1, :], ONE11[0:1, :], True, True, rd=[GRst, st_small], wrst=[PSs])
        act(GCOL[:, :], PS[0:8, 0:1], AF.Copy, rd=[PSs], wrst=[GCOLst])
        mm(PS[:, 1:2], B2M[:, :], GCOL[:, :], True, True, rd=[GCOLst, st_small], wrst=[PSs])
        act(B2S[:, :], PS[:, 1:2], AF.Copy, rd=[PSs], wrst=[B2Sst])
        mm(PS[:, 2:10], ORF[0:1, :], GR[0:1, :], True, True, rd=[GRst, st_small], wrst=[PSs])
        act(GB[:, :], PS[:, 2:10], AF.Copy, rd=[PSs], wrst=[GBst])
        for e in range(E):
            vts(W2S[:, e * 128:(e + 1) * 128], W2F[:, e * 128:(e + 1) * 128],
                GB[:, e:e + 1], AL.mult, rd=[st_w2f, GBst], wrst=[st_w2s])
        P.barrier()

        TE = [ARB[:, 0:512], ARB[:, 512:1024]]
        TEst = [BufSt(), BufSt()]
        XL = ARF[:, 0:512]; XLst = BufSt()
        for tg in range(NG):
            for h in range(H):
                xh = RES[:, h * T + tg * 512: h * T + tg * 512 + 512]
                xcell = RESst[h][tg]
                for e in range(E):
                    pp, pps = (PA, PAs) if e % 2 == 0 else (PB, PBs)
                    mm(pp[:, :], W1B[:, e * 128:(e + 1) * 128], xh, True, True,
                       rd=[st_w1, xcell], wrst=[pps])
                    act(TE[e % 2], pp[:, :], AF.Gelu, rd=[pps, st_small],
                        wrst=[TEst[e % 2]], bias=B1T[:, e:e + 1], scale=1.0)
                    mm(PC[:, :], W2S[:, e * 128:(e + 1) * 128], TE[e % 2],
                       e == 0, e == E - 1, rd=[st_w2s, TEst[e % 2]], wrst=[PCs])
                vts(XL, xh, B2S[:, 0:1], AL.add, rd=[xcell, B2Sst], wrst=[XLst])
                vtt(xh, PC[:, :], XL, AL.add, rd=[PCs, XLst], wrst=[xcell])
        P.barrier()
        fm_proj(w3T, list(range(16)), make_proj_epi(xsrc, xdst, B3C))
        P.barrier()

    def res_cells_route(k):
        return [RESst[k][tgi] for tgi in range(4)]

    ffn_stage(xs1, xs2)

    ln_pass(xs2)
    P.barrier()

    COS = ARB[:, 7680:11776]; COSst = BufSt()
    SIN = ARB[:, 11776:15872]; SINst = BufSt()
    XBR = ARB[:, 6144:6656]; XBRst = BufSt()
    XSW = ARB[:, 6656:7168]; XSWst = BufSt()
    T0 = ARB[:, 7168:7680]; T0st = BufSt()
    dma("sync", COS, c_cos, wr=[COSst])
    dma("sync", SIN, c_sin, wr=[SINst])

    def b1_epi(j, tg, pp, pps):
        i = tg % 2
        cs = COS[:, tg * 512:(tg + 1) * 512]
        sn = SIN[:, tg * 512:(tg + 1) * 512]
        act(XBR, pp[:, :], AF.Copy, rd=[pps], wrst=[XBRst])
        mm(PD[:, :], SWP[:, :], XBR, True, True, rd=[st_id, XBRst], wrst=[PDs])
        act(XSW, PD[:, :], AF.Copy, rd=[PDs], wrst=[XSWst])
        vtt(QS[i], XBR, cs, AL.mult, rd=[XBRst, COSst], wrst=[QSst[i]])
        vtt(T0, XSW, sn, AL.mult, rd=[XSWst, SINst], wrst=[T0st])
        vtt(QS[i], QS[i], T0, AL.add, rd=[QSst[i], T0st], wrst=[QSst[i]])
        dma("scalar", qkf2[j * 128:(j + 1) * 128, tg * 512:(tg + 1) * 512],
            QS[i], rd=[QSst[i]])

    fm_proj(watT, list(range(32)), b1_epi)
    P.barrier()
    tm_proj(watT[:, 2 * D:3 * D], vtm2, None)
    P.barrier()

    KH = ARB[:, 0:4096]; KHst = BufSt()
    VH = ARB[:, 4096:8192]; VHst = BufSt()
    QB4 = [ARB[:, 8192:8704], ARB[:, 8704:9216]]
    QB4st = [BufSt(), BufSt()]
    ES = [ARB[:, 9216:9728], ARB[:, 9728:10240]]
    ESst = [BufSt(), BufSt()]
    STR = ARB[:, 10240:12288]; STRst = BufSt()
    RBB = ARF[:, 0:512]; RBBst = BufSt()
    dma("sync", STR.rearrange("p (j m) -> p j m", m=512),
        c_stairs.rearrange("(j p) m -> p j m", p=128), wr=[STRst])

    for h in range(H):
        dma("sync", KH, qkf2[D + h * 128: D + (h + 1) * 128, :], wr=[KHst])
        dma("sync", VH.rearrange("p (kb d) -> p kb d", d=128),
            vtm2[:, h * 128:(h + 1) * 128].rearrange("(kb p) d -> p kb d", p=128),
            wr=[VHst])
        for QB in range(8):
            qi = QB % 2
            dma("sync", QB4[qi], qkf2[h * 128:(h + 1) * 128, QB * 512:(QB + 1) * 512],
                wr=[QB4st[qi]])
            nkb = 4 * QB + 4
            for kb in range(nkb):
                pp, pps = (PA, PAs) if kb % 2 == 0 else (PB, PBs)
                ei = kb % 2
                mm(pp[:, :], KH[:, kb * 128:(kb + 1) * 128], QB4[qi], True, True,
                   rd=[KHst, QB4st[qi]], wrst=[pps])
                act(ES[ei], pp[:, :], AF.Exp, rd=[pps, st_small], wrst=[ESst[ei]],
                    bias=NEGS[:, 0:1], scale=ISQ)
                j = kb - 4 * QB
                if j >= 0:
                    vtt(ES[ei], ES[ei], STR[:, j * 512:(j + 1) * 512], AL.mult,
                        rd=[ESst[ei], STRst], wrst=[ESst[ei]])
                mm(PR1[0:1, :], OCB[:, :], ES[ei], kb == 0, kb == nkb - 1,
                   rd=[ESst[ei], st_small], wrst=[PR1s])
                mm(PC[:, :], VH[:, kb * 128:(kb + 1) * 128], ES[ei],
                   kb == 0, kb == nkb - 1, rd=[VHst, ESst[ei]], wrst=[PCs])
            P.em("vector", (lambda o=RRC[0:1, :], a=PR1[0:1, :]: nc.vector.reciprocal(o, a)),
                 reads=[PR1s], writes=[RRCst])
            mm(PD[:, :], ORF[0:1, :], RRC[0:1, :], True, True, rd=[RRCst, st_small], wrst=[PDs])
            act(RBB, PD[:, :], AF.Copy, rd=[PDs], wrst=[RBBst])
            vtt(RES[:, h * T + QB * 512: h * T + QB * 512 + 512], PC[:, :], RBB,
                AL.mult, rd=[PCs, RBBst], wrst=[RESst[h][QB]])
    P.barrier()

    fm_proj(wprojT, list(range(16)), make_proj_epi(xs2, xs3, None))
    P.barrier()

    ffn_stage(xs3, out)
    P.barrier()

    semnames = sorted(P.cnt.keys())
    sems = {s: stack.enter_context(nc.semaphore(s)) for s in semnames}

    with nc.Block() as block:
        def runner(ename):
            def _run(eng):
                for waits, fn, sem, inc in P.ops[ename]:
                    for (s, v) in waits:
                        eng.wait_ge(sems[s], v)
                    if fn is not None:
                        inst = fn()
                        if inst is not None and sem is not None:
                            inst.then_inc(sems[sem], inc)
            return _run
        block.tensor(runner("tensor"))
        block.vector(runner("vector"))
        block.scalar(runner("scalar"))
        block.gpsimd(runner("gpsimd"))
        block.sync(runner("sync"))

    stack.close()
    return nc


_NC_CACHE = {}


def kernel(L0_attn_a_x, L0_attn_a_w_qkv, L0_attn_a_b_qkv, L0_attn_a_w_out,
           L0_attn_a_b_out, L0_attn_b_w_atten, L0_attn_b_w_proj,
           L0_ffn_switch_w, L0_ffn_switch_b, L0_ffn_w1, L0_ffn_b1,
           L0_ffn_w2, L0_ffn_b2, L0_ffn_w3, L0_ffn_b3):
    x = np.asarray(L0_attn_a_x, dtype=np.float32)
    wqkv = np.asarray(L0_attn_a_w_qkv, dtype=np.float32)
    bqkv = np.asarray(L0_attn_a_b_qkv, dtype=np.float32)
    wout = np.asarray(L0_attn_a_w_out, dtype=np.float32)
    bout = np.asarray(L0_attn_a_b_out, dtype=np.float32)
    watt = np.asarray(L0_attn_b_w_atten, dtype=np.float32)
    wproj = np.asarray(L0_attn_b_w_proj, dtype=np.float32)
    sw = np.asarray(L0_ffn_switch_w, dtype=np.float32)
    swb = np.asarray(L0_ffn_switch_b, dtype=np.float32)
    w1 = np.asarray(L0_ffn_w1, dtype=np.float32)
    b1 = np.asarray(L0_ffn_b1, dtype=np.float32)
    w2 = np.asarray(L0_ffn_w2, dtype=np.float32)
    b2 = np.asarray(L0_ffn_b2, dtype=np.float32)
    w3 = np.asarray(L0_ffn_w3, dtype=np.float32)
    b3 = np.asarray(L0_ffn_b3, dtype=np.float32)

    wqkvT = np.ascontiguousarray(wqkv.T)
    bqkv_cols = np.ascontiguousarray(bqkv.reshape(48, 128).T[:, 0:32])
    bout_cols = np.ascontiguousarray(bout.reshape(16, 128).T)
    b3_cols = np.ascontiguousarray(b3.reshape(16, 128).T)
    perm = np.concatenate([np.arange(0, 128, 2), np.arange(1, 128, 2)])
    wp = watt.copy()
    for h in range(H):
        wp[h * 128:(h + 1) * 128] = watt[h * 128:(h + 1) * 128][perm]
        wp[D + h * 128:D + (h + 1) * 128] = watt[D + h * 128:D + (h + 1) * 128][perm]
    watT = np.ascontiguousarray(wp.T)
    sw_route = np.ascontiguousarray(
        sw.reshape(E, ROUTE_T, KT, 128).transpose(0, 3, 2, 1).reshape(E, 128, 32768))
    w1Ta = np.ascontiguousarray(w1.transpose(0, 2, 1)).reshape(E * 128, 128)
    w2Ta = np.ascontiguousarray(w2.transpose(0, 2, 1)).reshape(E * 128, 128)
    b1Ta = np.ascontiguousarray(b1.T)
    w3Ta = np.ascontiguousarray(w3.T)

    consts = make_consts()
    common = {
        "wqkvT": wqkvT, "bqkv": np.ascontiguousarray(bqkv[2 * D:3 * D].reshape(1, D)),
        "bqkv_cols": bqkv_cols,
        "woutT": np.ascontiguousarray(wout.T), "bout_cols": bout_cols,
        "watT": watT, "wprojT": np.ascontiguousarray(wproj.T),
        "sw_route": sw_route, "swb": np.ascontiguousarray(swb.reshape(1, E)),
        "w1T": w1Ta, "w2T": w2Ta, "b1T": b1Ta, "b2m": b2,
        "w3T": w3Ta, "b3_cols": b3_cols,
        "c_id": consts["identity"], "c_ocf": consts["ones_col_f"],
        "c_ocb": consts["ones_col_b"], "c_orf": consts["ones_row_f"],
        "c_one11": consts["one11"],
        "c_dmaskT": np.ascontiguousarray(consts["dmaskT"].reshape(H * 128, 128)),
        "c_decayq": np.ascontiguousarray(consts["decayq"].reshape(H * 128, 128)),
        "c_decayk": np.ascontiguousarray(consts["decayk"].reshape(H * 128, 128)),
        "c_cos": consts["ropecos"], "c_sin": consts["ropesin"],
        "c_swp": consts["swap"],
        "c_stairs": np.ascontiguousarray(consts["stairs"].reshape(4 * 128, 512)),
        "c_negs": np.full((128, 1), EXP_SHIFT, dtype=np.float32),
    }
    in_maps = []
    for b in range(8):
        m = dict(common)
        m["xin"] = np.ascontiguousarray(x[b].T)
        in_maps.append(m)

    if "nc" not in _NC_CACHE:
        _NC_CACHE["nc"] = build_nc()
    nc = _NC_CACHE["nc"]
    res = run_bass_kernel_spmd(nc, in_maps, core_ids=list(range(8)))
    outs = [np.ascontiguousarray(np.asarray(r["out"], dtype=np.float32).T)
            for r in res.results]
    return np.stack(outs, axis=0)


if __name__ == "__main__":
    import reference as _r
    inp = _r.setup_inputs(0)
    o = kernel(**{k: np.asarray(v) for k, v in inp.items()})
    print(o.shape, o.dtype)


# version 07 (final)
# speedup vs baseline: 1.5526x; 1.5526x over previous
import math
import numpy as np
import ml_dtypes
import concourse.bass as bass
import concourse.mybir as mybir
from concourse.bass_utils import run_bass_kernel_spmd

F32 = mybir.dt.float32
BF16 = mybir.dt.bfloat16
AL = mybir.AluOpType
AF = mybir.ActivationFunctionType
AX = mybir.AxisListType

D = 2048
T = 4096
H = 16
KT = 16
NG = 8
E = 8
ROUTE_T = 2048
EPS = 1e-5
ISQ = 1.0 / math.sqrt(128.0)
EXP_SHIFT = -8.0

BF = np.dtype(ml_dtypes.bfloat16)


class BufSt:
    __slots__ = ("w", "r")

    def __init__(self):
        self.w = None
        self.r = []


class Prog:
    ENGINES = ["tensor", "vector", "scalar", "gpsimd", "sync"]

    def __init__(self):
        self.ops = {e: [] for e in self.ENGINES}
        self.cnt = {}
        self.hw = {e: {} for e in self.ENGINES}

    def em(self, eng, fn, reads=(), writes=(), dma=False):
        deps = []
        for b in reads:
            if b.w is not None:
                deps.append(b.w)
        for b in writes:
            deps.extend(b.r)
            if b.w is not None:
                deps.append(b.w)
        need = {}
        for (s, v) in deps:
            if v > need.get(s, 0):
                need[s] = v
        hw = self.hw[eng]
        waits = []
        for s, v in need.items():
            if v > hw.get(s, 0):
                hw[s] = v
                waits.append((s, v))
        sem = ("q_" + eng) if dma else ("c_" + eng)
        inc = 16 if dma else 1
        self.cnt[sem] = self.cnt.get(sem, 0) + inc
        tok = (sem, self.cnt[sem])
        self.ops[eng].append((waits, fn, sem, inc))
        for b in reads:
            b.r.append(tok)
        for b in writes:
            b.w = tok
            b.r = []
        return tok

    def barrier(self):
        cur = dict(self.cnt)
        for e in self.ENGINES:
            hw = self.hw[e]
            waits = []
            for s, v in cur.items():
                if v > hw.get(s, 0):
                    hw[s] = v
                    waits.append((s, v))
            if waits:
                self.ops[e].append((waits, None, None, 0))


def make_consts():
    c = {}
    c["identity"] = np.eye(128, dtype=np.float32).astype(BF)
    c["ones_col_f"] = np.ones((128, 1), dtype=np.float32)
    c["ones_col_b"] = np.ones((128, 1), dtype=np.float32).astype(BF)
    c["ones_row_f"] = np.ones((1, 128), dtype=np.float32)
    c["one11"] = np.ones((1, 1), dtype=np.float32)
    start = 2.0 ** (-2.0 ** (-(math.log2(H) - 3)))
    slopes = np.array([start ** (i + 1) for i in range(H)], dtype=np.float64)
    lam = np.exp(-slopes)
    pos = np.arange(128, dtype=np.float64)
    dmaskT = np.zeros((H, 128, 128), dtype=np.float64)
    dq = np.zeros((H, 128, 128), dtype=np.float64)
    dk = np.zeros((H, 128, 128), dtype=np.float64)
    for h in range(H):
        diff = pos[None, :] - pos[:, None]
        dmaskT[h] = np.where(diff >= 0, lam[h] ** np.maximum(diff, 0), 0.0)
        dq[h][:, :] = (lam[h] ** (pos + 1.0))[None, :]
        dk[h][:, :] = (lam[h] ** (127.0 - pos))[:, None]
    c["dmaskT"] = dmaskT.astype(np.float32).astype(BF)
    c["decayq"] = dq.astype(np.float32).astype(BF)
    c["decayk"] = dk.astype(np.float32).astype(BF)
    c["lam_c"] = [float(l) ** 128 for l in lam]
    theta = 1.0 / (10000.0 ** (np.arange(0, 128, 2, dtype=np.float64) / 128.0))
    ang = np.outer(theta, np.arange(T, dtype=np.float64))
    cs_ = np.cos(ang).astype(np.float32)
    sn_ = np.sin(ang).astype(np.float32)
    c["ropecos"] = np.vstack([cs_, cs_]).astype(BF)
    c["ropesin"] = np.vstack([-sn_, sn_]).astype(BF)
    c["swap"] = np.roll(np.eye(128, dtype=np.float32), 64, axis=0).astype(BF)
    stairs = np.zeros((4, 128, 512), dtype=np.float32)
    tkv = np.arange(128)[:, None]
    tqv = np.arange(512)[None, :]
    for j in range(4):
        stairs[j] = (tqv >= j * 128 + tkv).astype(np.float32)
    c["stairs"] = stairs.astype(BF)
    return c


def build_nc():
    nc = bass.Bass("TRN2", target_bir_lowering=False)
    P = Prog()

    xin = nc.dram_tensor("xin", [D, T], F32, kind="ExternalInput").ap()
    wqkvT = nc.dram_tensor("wqkvT", [D, 3 * D], F32, kind="ExternalInput").ap()
    bqkv = nc.dram_tensor("bqkv", [1, D], F32, kind="ExternalInput").ap()
    bqkv_cols = nc.dram_tensor("bqkv_cols", [128, 32], F32, kind="ExternalInput").ap()
    woutT = nc.dram_tensor("woutT", [D, D], F32, kind="ExternalInput").ap()
    bout_cols = nc.dram_tensor("bout_cols", [128, 16], F32, kind="ExternalInput").ap()
    watT = nc.dram_tensor("watT", [D, 3 * D], F32, kind="ExternalInput").ap()
    wprojT = nc.dram_tensor("wprojT", [D, D], F32, kind="ExternalInput").ap()
    sw_route = nc.dram_tensor("sw_route", [E, 128, 32768], F32, kind="ExternalInput").ap()
    swb = nc.dram_tensor("swb", [1, E], F32, kind="ExternalInput").ap()
    w1T = nc.dram_tensor("w1T", [E * 128, 128], F32, kind="ExternalInput").ap()
    w2T = nc.dram_tensor("w2T", [E * 128, 128], F32, kind="ExternalInput").ap()
    b1T_d = nc.dram_tensor("b1T", [128, E], F32, kind="ExternalInput").ap()
    b2m_d = nc.dram_tensor("b2m", [E, 128], F32, kind="ExternalInput").ap()
    w3T = nc.dram_tensor("w3T", [D, D], F32, kind="ExternalInput").ap()
    b3_cols = nc.dram_tensor("b3_cols", [128, 16], F32, kind="ExternalInput").ap()

    c_id = nc.dram_tensor("c_id", [128, 128], BF16, kind="ExternalInput").ap()
    c_ocf = nc.dram_tensor("c_ocf", [128, 1], F32, kind="ExternalInput").ap()
    c_ocb = nc.dram_tensor("c_ocb", [128, 1], BF16, kind="ExternalInput").ap()
    c_orf = nc.dram_tensor("c_orf", [1, 128], F32, kind="ExternalInput").ap()
    c_one11 = nc.dram_tensor("c_one11", [1, 1], F32, kind="ExternalInput").ap()
    c_dmaskT = nc.dram_tensor("c_dmaskT", [H * 128, 128], BF16, kind="ExternalInput").ap()
    c_decayq = nc.dram_tensor("c_decayq", [H * 128, 128], BF16, kind="ExternalInput").ap()
    c_decayk = nc.dram_tensor("c_decayk", [H * 128, 128], BF16, kind="ExternalInput").ap()
    c_cos = nc.dram_tensor("c_cos", [128, T], BF16, kind="ExternalInput").ap()
    c_sin = nc.dram_tensor("c_sin", [128, T], BF16, kind="ExternalInput").ap()
    c_swp = nc.dram_tensor("c_swp", [128, 128], BF16, kind="ExternalInput").ap()
    c_stairs = nc.dram_tensor("c_stairs", [4 * 128, 512], BF16, kind="ExternalInput").ap()
    c_negs = nc.dram_tensor("c_negs", [128, 1], F32, kind="ExternalInput").ap()

    out = nc.dram_tensor("out", [D, T], F32, kind="ExternalOutput").ap()
    xs1 = nc.dram_tensor("xs1", [D, T], F32, kind="Internal").ap()
    xs2 = nc.dram_tensor("xs2", [D, T], F32, kind="Internal").ap()
    xs3 = nc.dram_tensor("xs3", [D, T], F32, kind="Internal").ap()
    qkf = nc.dram_tensor("qkf", [2 * D, T], BF16, kind="Internal").ap()
    vtm = nc.dram_tensor("vtm", [T, D], BF16, kind="Internal").ap()
    qkf2 = nc.dram_tensor("qkf2", [2 * D, T], BF16, kind="Internal").ap()
    vtm2 = nc.dram_tensor("vtm2", [T, D], BF16, kind="Internal").ap()

    import contextlib
    stack = contextlib.ExitStack()

    def sb(name, shape, dt):
        return stack.enter_context(nc.sbuf_tensor(name, shape, dt)).ap()

    def pst(name, shape, dt=F32):
        return stack.enter_context(nc.psum_tensor(name, shape, dt)).ap()

    lam_c = make_consts()["lam_c"]

    RES = sb("RES", [128, KT * T], BF16)
    ARF = sb("ARF", [128, 6144], F32)
    ARB = sb("ARB", [128, 15872], BF16)
    ID = sb("ID", [128, 128], BF16)
    SWP = sb("SWP", [128, 128], BF16)
    OCF = sb("OCF", [128, 1], F32)
    OCB = sb("OCB", [128, 1], BF16)
    ORF = sb("ORF", [1, 128], F32)
    ONE11 = sb("ONE11", [1, 1], F32)
    BQC = sb("BQC", [128, 32], F32)
    BOC = sb("BOC", [128, 16], F32)
    B3C = sb("B3C", [128, 16], F32)
    BQROW = sb("BQROW", [1, D], F32)
    SWB = sb("SWB", [1, E], F32)
    B1T = sb("B1T", [128, E], F32)
    B2M = sb("B2M", [8, 128], F32)
    W1B = sb("W1B", [128, E * 128], BF16)
    W2F = sb("W2F", [128, E * 128], F32)
    W2S = sb("W2S", [128, E * 128], BF16)
    MU = sb("MU", [1, 512], F32)
    M2 = sb("M2", [1, 512], F32)
    LG = sb("LG", [1, 8], F32)
    PRB = sb("PRB", [1, 8], F32)
    SC1 = sb("SC1", [1, 1], F32)
    SC2 = sb("SC2", [1, 1], F32)
    GR = sb("GR", [1, 8], F32)
    GCOL = sb("GCOL", [8, 1], F32)
    B2S = sb("B2S", [128, 1], F32)
    GB = sb("GB", [128, 8], F32)
    RACC = sb("RACC", [128, 8], F32)
    RCA = sb("RCA", [128, 1], F32)
    RCB = sb("RCB", [128, 1], F32)
    RRC = sb("RRC", [1, 512], F32)
    RS = RRC
    NEGS = sb("NEGS", [128, 1], F32)

    RESst = [[BufSt() for _ in range(NG)] for _ in range(KT)]
    st_id = BufSt(); st_small = BufSt()
    st_w1 = BufSt(); st_w2f = BufSt(); st_w2s = BufSt()
    MUst = BufSt(); M2st = BufSt(); RSst = BufSt()
    LGst = BufSt(); PRBst = BufSt(); SC1st = BufSt(); SC2st = BufSt()
    GRst = BufSt(); GCOLst = BufSt(); B2Sst = BufSt(); GBst = BufSt()
    RACCst = BufSt(); RCAst = BufSt(); RCBst = BufSt(); RRCst = BufSt()

    PA = pst("PA", [128, 512]); PAs = BufSt()
    PB = pst("PB", [128, 512]); PBs = BufSt()
    PC = pst("PC", [128, 512]); PCs = BufSt()
    PD = pst("PD", [128, 512]); PDs = BufSt()
    PS = pst("PS", [128, 128]); PSs = BufSt()
    PT = pst("PT", [128, 128], BF16); PTs = BufSt()
    PR = pst("PR", [64, 512])
    PR1 = PR[0:1, :]; PR1s = BufSt()
    PR2 = PR[32:33, :]; PR2s = BufSt()
    PEX = pst("PEX", [128, 512]); PEXs = BufSt()

    def dma(eng, dst, src, rd=(), wr=(), cast=False):
        q = "gpsimd" if cast else eng
        return P.em(q, (lambda d=dst, s=src, qq=q: getattr(nc, qq).dma_start(out=d, in_=s)),
                    reads=rd, writes=wr, dma=True)

    dma("sync", ID[:, :], c_id, wr=[st_id])
    dma("sync", SWP[:, :], c_swp, wr=[st_id])
    dma("sync", OCF[:, :], c_ocf, wr=[st_small])
    dma("sync", OCB[:, :], c_ocb, wr=[st_small])
    dma("sync", ORF[:, :], c_orf, wr=[st_small])
    dma("sync", ONE11[:, :], c_one11, wr=[st_small])
    dma("sync", BQC[:, :], bqkv_cols, wr=[st_small])
    dma("sync", BOC[:, :], bout_cols, wr=[st_small])
    dma("sync", B3C[:, :], b3_cols, wr=[st_small])
    dma("sync", BQROW[:, :], bqkv, wr=[st_small])
    dma("sync", SWB[:, :], swb, wr=[st_small])
    dma("sync", B1T[:, :], b1T_d, wr=[st_small])
    dma("sync", B2M[:, :], b2m_d, wr=[st_small])
    dma("sync", NEGS[:, :], c_negs, wr=[st_small])
    dma("sync", W1B.rearrange("p (e m) -> p e m", m=128),
        w1T.rearrange("(e p) m -> p e m", p=128), wr=[st_w1], cast=True)
    dma("sync", W2F.rearrange("p (e m) -> p e m", m=128),
        w2T.rearrange("(e p) m -> p e m", p=128), wr=[st_w2f])
    P.barrier()

    def mm(outap, lhsT, rhs, start, stop, rd, wrst):
        return P.em("tensor",
                    (lambda o=outap, l=lhsT, r=rhs, s0=start, s1=stop:
                     nc.tensor.matmul(o, l, r, start=s0, stop=s1)),
                    reads=rd, writes=wrst)

    def act(outap, inap, func, rd, wrst, bias=None, scale=None):
        kw = {}
        if bias is not None:
            kw["bias"] = bias
        if scale is not None:
            kw["scale"] = scale
        return P.em("scalar",
                    (lambda o=outap, i=inap, f=func, k=dict(kw):
                     nc.scalar.activation(o, i, f, **k)),
                    reads=rd, writes=wrst)

    def vtt(outap, in0, in1, op, rd, wrst):
        return P.em("vector",
                    (lambda o=outap, a=in0, b=in1, p=op:
                     nc.vector.tensor_tensor(o, a, b, op=p)),
                    reads=rd, writes=wrst)

    def vts(outap, in0, s1, op, rd, wrst):
        return P.em("vector",
                    (lambda o=outap, a=in0, s=s1, p=op:
                     nc.vector.tensor_scalar(o, a, s, None, p)),
                    reads=rd, writes=wrst)

    def vred(outap, inap, op, rd, wrst):
        return P.em("vector",
                    (lambda o=outap, i=inap, p=op:
                     nc.vector.tensor_reduce(o, i, AX.X, p)),
                    reads=rd, writes=wrst)

    XT = [ARF[:, 0:512], ARF[:, 512:1024]]
    XTst = [BufSt(), BufSt()]
    LNR = ARF[:, 4608:5120]; LNRst = BufSt()
    XB_ = ARB[:, 0:512]; XBst = BufSt()
    SQ_ = ARB[:, 512:1024]; SQst = BufSt()

    def ln_pass(xsrc):
        for tg in range(NG):
            for k in range(KT):
                i = k % 2
                dma("sync", XT[i], xsrc[k * 128:(k + 1) * 128, tg * 512:(tg + 1) * 512],
                    wr=[XTst[i]])
                act(XB_, XT[i], AF.Copy, rd=[XTst[i]], wrst=[XBst])
                act(SQ_, XT[i], AF.Square, rd=[XTst[i]], wrst=[SQst])
                mm(PR1, OCB[:, :], XB_, k == 0, k == KT - 1,
                   rd=[XBst, st_small], wrst=[PR1s])
                mm(PR2, OCB[:, :], SQ_, k == 0, k == KT - 1,
                   rd=[SQst, st_small], wrst=[PR2s])
            act(MU[0:1, :], PR1, AF.Copy, rd=[PR1s], wrst=[MUst], scale=1.0 / D)
            act(M2[0:1, :], PR2, AF.Copy, rd=[PR2s], wrst=[M2st], scale=1.0 / D)
            vtt(RS[0:1, :], MU[0:1, :], MU[0:1, :], AL.mult, rd=[MUst], wrst=[RSst])
            vtt(M2[0:1, :], M2[0:1, :], RS[0:1, :], AL.subtract, rd=[M2st, RSst], wrst=[M2st])
            vts(M2[0:1, :], M2[0:1, :], EPS, AL.add, rd=[M2st], wrst=[M2st])
            act(RS[0:1, :], M2[0:1, :], AF.Sqrt, rd=[M2st], wrst=[RSst])
            mm(PEX[:, :], ORF[0:1, :], RS[0:1, :], True, True, rd=[RSst, st_small], wrst=[PEXs])
            P.em("vector", (lambda o=LNR, i=PEX[:, :]: nc.vector.reciprocal(o, i)),
                 reads=[PEXs], writes=[LNRst])
            mm(PEX[:, :], ORF[0:1, :], MU[0:1, :], True, True, rd=[MUst, st_small], wrst=[PEXs])
            for k in range(KT):
                i = k % 2
                dma("sync", XT[i], xsrc[k * 128:(k + 1) * 128, tg * 512:(tg + 1) * 512],
                    wr=[XTst[i]])
                vtt(XT[i], XT[i], PEX[:, :], AL.subtract, rd=[XTst[i], PEXs], wrst=[XTst[i]])
                vtt(RES[:, k * T + tg * 512: k * T + tg * 512 + 512], XT[i], LNR,
                    AL.mult, rd=[XTst[i], LNRst], wrst=[RESst[k][tg]])

    WT = [ARB[:, 1024:3072], ARB[:, 3072:5120]]
    WTst = [BufSt(), BufSt()]

    def fm_proj(wsrc, jlist, epi, pre=None):
        PP = [PA, PB, PC, PD]
        PPst = [PAs, PBs, PCs, PDs]
        for idx, j in enumerate(jlist):
            i = idx % 2
            dma("sync", WT[i].rearrange("p (k m) -> p k m", m=128),
                wsrc[:, j * 128:(j + 1) * 128].rearrange("(k p) m -> p k m", p=128),
                wr=[WTst[i]], cast=True)
            for half in range(2):
                tgs = [half * 4 + t for t in range(4)]
                if pre is not None:
                    for tg in tgs:
                        pre(j, tg)
                for k in range(KT):
                    for t4, tg in enumerate(tgs):
                        mm(PP[t4][:, :], WT[i][:, k * 128:(k + 1) * 128],
                           RES[:, k * T + tg * 512: k * T + tg * 512 + 512],
                           k == 0, k == KT - 1,
                           rd=[WTst[i], RESst[k][tg]], wrst=[PPst[t4]])
                for t4, tg in enumerate(tgs):
                    epi(j, tg, PP[t4], PPst[t4])

    def tm_proj(wsrc, dst, vbias):
        WVG = ARB[:, 1024:9216]
        WVGst = BufSt()
        OT = [ARB[:, 9216:9728], ARB[:, 9728:10240]]
        OTst = [BufSt(), BufSt()]
        VB = ARF[:, 1024:1536]; VBst = BufSt()
        for g in range(4):
            dma("sync", WVG.rearrange("p (k m) -> p k m", m=512),
                wsrc[:, g * 512:(g + 1) * 512].rearrange("(k p) m -> p k m", p=128),
                wr=[WVGst], cast=True)
            if vbias is not None:
                mm(PD[:, :], ORF[0:1, :], vbias[0:1, g * 512:(g + 1) * 512], True, True,
                   rd=[st_small], wrst=[PDs])
                act(VB, PD[:, :], AF.Copy, rd=[PDs], wrst=[VBst])
            for c in range(T // 128):
                pp, pps = (PA, PAs) if c % 2 == 0 else (PB, PBs)
                tgc, off = c // 4, (c % 4) * 128
                for k in range(KT):
                    mm(pp[:, :], RES[:, k * T + tgc * 512 + off: k * T + tgc * 512 + off + 128],
                       WVG[:, k * 512:(k + 1) * 512], k == 0, k == KT - 1,
                       rd=[WVGst, RESst[k][tgc]], wrst=[pps])
                oi = c % 2
                if vbias is not None:
                    vtt(OT[oi], pp[:, :], VB, AL.add, rd=[pps, VBst], wrst=[OTst[oi]])
                else:
                    act(OT[oi], pp[:, :], AF.Copy, rd=[pps], wrst=[OTst[oi]])
                dma("scalar", dst[c * 128:(c + 1) * 128, g * 512:(g + 1) * 512],
                    OT[oi], rd=[OTst[oi]])

    XR = [ARF[:, 1024 + 512 * i: 1536 + 512 * i] for i in range(4)]
    XRst = [BufSt() for _ in range(4)]
    TA = [ARF[:, 3072:3584], ARF[:, 3584:4096]]
    TAst = [BufSt(), BufSt()]

    def make_proj_epi(xres, xdst, bias_cols):
        def pre(j, tg):
            i = tg % 4
            dma("sync", XR[i], xres[j * 128:(j + 1) * 128, tg * 512:(tg + 1) * 512],
                wr=[XRst[i]])
        def epi(j, tg, pp, pps):
            i = tg % 4
            ti = tg % 2
            if bias_cols is not None:
                vts(TA[ti], pp[:, :], bias_cols[:, j:j + 1], AL.add,
                    rd=[pps, st_small], wrst=[TAst[ti]])
            else:
                act(TA[ti], pp[:, :], AF.Copy, rd=[pps], wrst=[TAst[ti]])
            vtt(TA[ti], TA[ti], XR[i], AL.add, rd=[TAst[ti], XRst[i]], wrst=[TAst[ti]])
            dma("scalar", xdst[j * 128:(j + 1) * 128, tg * 512:(tg + 1) * 512],
                TA[ti], rd=[TAst[ti]])
        return pre, epi

    ln_pass(xin)

    QS = [ARB[:, 5120:5632], ARB[:, 5632:6144]]
    QSst = [BufSt(), BufSt()]

    def a1_epi(j, tg, pp, pps):
        i = tg % 2
        vts(QS[i], pp[:, :], BQC[:, j:j + 1], AL.add, rd=[pps, st_small],
            wrst=[QSst[i]])
        dma("scalar", qkf[j * 128:(j + 1) * 128, tg * 512:(tg + 1) * 512],
            QS[i], rd=[QSst[i]])

    fm_proj(wqkvT, list(range(32)), a1_epi)
    P.barrier()
    tm_proj(wqkvT[:, 2 * D:3 * D], vtm, BQROW)
    P.barrier()

    QC = ARB[:, 0:2048]; QCst = BufSt()
    KC = ARB[:, 2048:4096]; KCst = BufSt()
    VC = ARB[:, 4096:6144]; VCst = BufSt()
    DMK = ARB[:, 6144:8192]; DMKst = BufSt()
    DCQ = ARB[:, 8192:10240]; DCQst = BufSt()
    DCK = ARB[:, 10240:12288]; DCKst = BufSt()
    SBF = ARB[:, 12288:14336]; SBFst = [BufSt() for _ in range(H)]
    SCB = ARB[:, 14336:14464]; SCBst = BufSt()
    STM = ARB[:, 14464:14592]; STMst = BufSt()
    QD = ARB[:, 14592:14720]; QDst = BufSt()
    KD = ARB[:, 14720:14848]; KDst = BufSt()
    SF = ARF[:, 0:2048]; SFst = [BufSt() for _ in range(H)]

    dma("sync", DMK.rearrange("p (h m) -> p h m", m=128),
        c_dmaskT.rearrange("(h p) m -> p h m", p=128), wr=[DMKst])
    dma("sync", DCQ.rearrange("p (h m) -> p h m", m=128),
        c_decayq.rearrange("(h p) m -> p h m", p=128), wr=[DCQst])
    dma("sync", DCK.rearrange("p (h m) -> p h m", m=128),
        c_decayk.rearrange("(h p) m -> p h m", p=128), wr=[DCKst])
    P.em("vector", (lambda o=SF: nc.vector.memset(o, 0.0)), writes=SFst)
    P.em("vector", (lambda o=SBF: nc.vector.memset(o, 0.0)), writes=SBFst)

    PC2st = [BufSt(), BufSt()]
    PD2st = [BufSt(), BufSt()]
    PS2st = [BufSt(), BufSt()]
    for c in range(T // 128):
        dma("sync", QC.rearrange("p (h m) -> p h m", m=128),
            qkf[0:D, c * 128:(c + 1) * 128].rearrange("(h p) m -> p h m", p=128),
            wr=[QCst])
        dma("sync", KC.rearrange("p (h m) -> p h m", m=128),
            qkf[D:2 * D, c * 128:(c + 1) * 128].rearrange("(h p) m -> p h m", p=128),
            wr=[KCst])
        dma("sync", VC, vtm[c * 128:(c + 1) * 128, :], wr=[VCst])

        def a2_score(h):
            hs = slice(h * 128, (h + 1) * 128)
            sp = PA[:, 0:128] if h % 2 == 0 else PB[:, 0:128]
            mm(sp, KC[:, hs], QC[:, hs], True, True, rd=[KCst, QCst],
               wrst=[PS2st[h % 2]])
        a2_score(0)
        for h in range(H):
            hs = slice(h * 128, (h + 1) * 128)
            sp = PA[:, 0:128] if h % 2 == 0 else PB[:, 0:128]
            spst = PS2st[h % 2]
            pc = PC[:, (h % 2) * 128:(h % 2) * 128 + 128]
            pcst = PC2st[h % 2]
            pd = PD[:, (h % 2) * 128:(h % 2) * 128 + 128]
            pdst = PD2st[h % 2]
            if h + 1 < H:
                a2_score(h + 1)
            act(SCB, sp, AF.Copy, rd=[spst], wrst=[SCBst])
            vtt(STM, SCB, DMK[:, hs], AL.mult, rd=[SCBst, DMKst], wrst=[STMst])
            mm(pc, VC[:, hs], STM, True, False, rd=[VCst, STMst], wrst=[pcst])
            vtt(QD, QC[:, hs], DCQ[:, hs], AL.mult, rd=[QCst, DCQst], wrst=[QDst])
            mm(pc, SBF[:, hs], QD, False, True, rd=[SBFst[h], QDst], wrst=[pcst])
            P.em("tensor", (lambda o=PT[:, :], a=KC[:, hs], b=ID[:, :]:
                            nc.tensor.transpose(o, a, b)),
                 reads=[KCst, st_id], writes=[PTs])
            vtt(KD, PT[:, :], DCK[:, hs], AL.mult, rd=[PTs, DCKst], wrst=[KDst])
            mm(pd, KD, VC[:, hs], True, True, rd=[KDst, VCst], wrst=[pdst])
            vts(SF[:, hs], SF[:, hs], lam_c[h], AL.mult, rd=[SFst[h]], wrst=[SFst[h]])
            vtt(SF[:, hs], pd, SF[:, hs], AL.add, rd=[pdst, SFst[h]], wrst=[SFst[h]])
            act(SBF[:, hs], SF[:, hs], AF.Copy, rd=[SFst[h]], wrst=[SBFst[h]])
            act(RES[:, h * T + c * 128: h * T + c * 128 + 128], pc, AF.Copy,
                rd=[pcst], wrst=[RESst[h][c // 4]])
    P.barrier()

    _pr, _ep = make_proj_epi(xin, xs1, BOC)
    fm_proj(woutT, list(range(16)), _ep, pre=_pr)
    P.barrier()

    def ffn_stage(xsrc, xdst):
        ln_pass(xsrc)
        SWS = [ARB[:, 8192:10240], ARB[:, 10240:12288]]
        SWSst = [BufSt(), BufSt()]
        DUMP = ARB[:, 6144:8192]; DUMPst = BufSt()
        nseg = 16
        for e in range(E):
            for s in range(nseg):
                i = (e * nseg + s) % 2
                dma("sync", SWS[i], sw_route[e, :, s * 2048:(s + 1) * 2048],
                    wr=[SWSst[i]], cast=True)
                rin = RES[:, s * T: s * T + ROUTE_T]
                rdl = [SWSst[i]] + res_cells_route(s)
                vtt(DUMP, rin, SWS[i], AL.mult, rd=rdl, wrst=[DUMPst])
                if s == 0:
                    vred(RCA[:, :], DUMP, AL.add, rd=[DUMPst], wrst=[RCAst])
                else:
                    vred(RCB[:, :], DUMP, AL.add, rd=[DUMPst], wrst=[RCBst])
                    vtt(RCA[:, :], RCA[:, :], RCB[:, :], AL.add,
                        rd=[RCAst, RCBst], wrst=[RCAst])
            act(RACC[:, e:e + 1], RCA[:, :], AF.Copy, rd=[RCAst], wrst=[RACCst])
        mm(PS[0:1, 0:8], OCF[:, :], RACC[:, :], True, True, rd=[RACCst, st_small], wrst=[PSs])
        vtt(LG[0:1, :], PS[0:1, 0:8], SWB[0:1, :], AL.add, rd=[PSs, st_small], wrst=[LGst])
        vred(SC1[0:1, :], LG[0:1, :], AL.max, rd=[LGst], wrst=[SC1st])
        vts(SC1[0:1, :], SC1[0:1, :], -1.0, AL.mult, rd=[SC1st], wrst=[SC1st])
        act(PRB[0:1, :], LG[0:1, :], AF.Exp, rd=[LGst, SC1st], wrst=[PRBst],
            bias=SC1[0:1, 0:1], scale=1.0)
        vred(SC2[0:1, :], PRB[0:1, :], AL.add, rd=[PRBst], wrst=[SC2st])
        P.em("vector", (lambda o=SC2[0:1, :], i=SC2[0:1, :]: nc.vector.reciprocal(o, i)),
             reads=[SC2st], writes=[SC2st])
        vts(PRB[0:1, :], PRB[0:1, :], SC2[0:1, 0:1], AL.mult, rd=[PRBst, SC2st], wrst=[PRBst])
        vred(SC1[0:1, :], PRB[0:1, :], AL.max, rd=[PRBst], wrst=[SC1st])
        vts(GR[0:1, :], PRB[0:1, :], SC1[0:1, 0:1], AL.is_ge, rd=[PRBst, SC1st], wrst=[GRst])
        vtt(GR[0:1, :], GR[0:1, :], PRB[0:1, :], AL.mult, rd=[GRst, PRBst], wrst=[GRst])
        vtt(GR[0:1, :], PRB[0:1, :], GR[0:1, :], AL.subtract, rd=[GRst, PRBst], wrst=[GRst])
        vred(SC2[0:1, :], GR[0:1, :], AL.max, rd=[GRst], wrst=[SC2st])
        vts(GR[0:1, :], PRB[0:1, :], SC2[0:1, 0:1], AL.is_ge, rd=[PRBst, SC2st], wrst=[GRst])
        vtt(GR[0:1, :], GR[0:1, :], PRB[0:1, :], AL.mult, rd=[GRst, PRBst], wrst=[GRst])
        mm(PS[0:8, 0:1], GR[0:1, :], ONE11[0:1, :], True, True, rd=[GRst, st_small], wrst=[PSs])
        act(GCOL[:, :], PS[0:8, 0:1], AF.Copy, rd=[PSs], wrst=[GCOLst])
        mm(PS[:, 1:2], B2M[:, :], GCOL[:, :], True, True, rd=[GCOLst, st_small], wrst=[PSs])
        act(B2S[:, :], PS[:, 1:2], AF.Copy, rd=[PSs], wrst=[B2Sst])
        mm(PS[:, 2:10], ORF[0:1, :], GR[0:1, :], True, True, rd=[GRst, st_small], wrst=[PSs])
        act(GB[:, :], PS[:, 2:10], AF.Copy, rd=[PSs], wrst=[GBst])
        for e in range(E):
            vts(W2S[:, e * 128:(e + 1) * 128], W2F[:, e * 128:(e + 1) * 128],
                GB[:, e:e + 1], AL.mult, rd=[st_w2f, GBst], wrst=[st_w2s])

        TE = [ARB[:, 12288:12800], ARB[:, 12800:13312]]
        TEst = [BufSt(), BufSt()]
        XL = ARF[:, 4096:4608]; XLst = BufSt()
        def moe_mm1(e, xh, xcell):
            pp, pps = (PA, PAs) if e % 2 == 0 else (PB, PBs)
            mm(pp[:, :], W1B[:, e * 128:(e + 1) * 128], xh, True, True,
               rd=[st_w1, xcell], wrst=[pps])
        for tg in range(NG):
            for h in range(H):
                xh = RES[:, h * T + tg * 512: h * T + tg * 512 + 512]
                xcell = RESst[h][tg]
                vts(XL, xh, B2S[:, 0:1], AL.add, rd=[xcell, B2Sst], wrst=[XLst])
                moe_mm1(0, xh, xcell)
                for e in range(E):
                    pp, pps = (PA, PAs) if e % 2 == 0 else (PB, PBs)
                    if e + 1 < E:
                        moe_mm1(e + 1, xh, xcell)
                    act(TE[e % 2], pp[:, :], AF.Gelu, rd=[pps, st_small],
                        wrst=[TEst[e % 2]], bias=B1T[:, e:e + 1], scale=1.0)
                    mm(PC[:, :], W2S[:, e * 128:(e + 1) * 128], TE[e % 2],
                       e == 0, e == E - 1, rd=[st_w2s, TEst[e % 2]], wrst=[PCs])
                vtt(xh, PC[:, :], XL, AL.add, rd=[PCs, XLst], wrst=[xcell])
        P.barrier()
        _prw, _epw = make_proj_epi(xsrc, xdst, B3C)
        fm_proj(w3T, list(range(16)), _epw, pre=_prw)
        P.barrier()

    def res_cells_route(k):
        return [RESst[k][tgi] for tgi in range(4)]

    ffn_stage(xs1, xs2)

    ln_pass(xs2)

    COS = ARB[:, 7680:11776]; COSst = BufSt()
    SIN = ARB[:, 11776:15872]; SINst = BufSt()
    XBR = ARB[:, 6144:6656]; XBRst = BufSt()
    XSW = ARB[:, 6656:7168]; XSWst = BufSt()
    T0 = ARB[:, 7168:7680]; T0st = BufSt()
    dma("sync", COS, c_cos, wr=[COSst])
    dma("sync", SIN, c_sin, wr=[SINst])

    def b1_epi(j, tg, pp, pps):
        i = tg % 2
        cs = COS[:, tg * 512:(tg + 1) * 512]
        sn = SIN[:, tg * 512:(tg + 1) * 512]
        act(XBR, pp[:, :], AF.Copy, rd=[pps], wrst=[XBRst])
        mm(PD[:, :], SWP[:, :], XBR, True, True, rd=[st_id, XBRst], wrst=[PDs])
        act(XSW, PD[:, :], AF.Copy, rd=[PDs], wrst=[XSWst])
        vtt(QS[i], XBR, cs, AL.mult, rd=[XBRst, COSst], wrst=[QSst[i]])
        vtt(T0, XSW, sn, AL.mult, rd=[XSWst, SINst], wrst=[T0st])
        vtt(QS[i], QS[i], T0, AL.add, rd=[QSst[i], T0st], wrst=[QSst[i]])
        dma("scalar", qkf2[j * 128:(j + 1) * 128, tg * 512:(tg + 1) * 512],
            QS[i], rd=[QSst[i]])

    fm_proj(watT, list(range(32)), b1_epi)
    P.barrier()
    tm_proj(watT[:, 2 * D:3 * D], vtm2, None)
    P.barrier()

    KH = ARB[:, 0:4096]; KHst = BufSt()
    VH = ARB[:, 4096:8192]; VHst = BufSt()
    QB4 = [ARB[:, 8192:8704], ARB[:, 8704:9216]]
    QB4st = [BufSt(), BufSt()]
    ES = [ARB[:, 9216:9728], ARB[:, 9728:10240]]
    ESst = [BufSt(), BufSt()]
    STR = ARB[:, 10240:12288]; STRst = BufSt()
    RBB = ARF[:, 0:512]; RBBst = BufSt()
    dma("sync", STR.rearrange("p (j m) -> p j m", m=512),
        c_stairs.rearrange("(j p) m -> p j m", p=128), wr=[STRst])

    for h in range(H):
        dma("sync", KH, qkf2[D + h * 128: D + (h + 1) * 128, :], wr=[KHst])
        dma("sync", VH.rearrange("p (kb d) -> p kb d", d=128),
            vtm2[:, h * 128:(h + 1) * 128].rearrange("(kb p) d -> p kb d", p=128),
            wr=[VHst])
        for QB in range(8):
            qi = QB % 2
            dma("sync", QB4[qi], qkf2[h * 128:(h + 1) * 128, QB * 512:(QB + 1) * 512],
                wr=[QB4st[qi]])
            nkb = 4 * QB + 4

            def b2_score(kb):
                pp, pps = (PA, PAs) if kb % 2 == 0 else (PB, PBs)
                mm(pp[:, :], KH[:, kb * 128:(kb + 1) * 128], QB4[qi], True, True,
                   rd=[KHst, QB4st[qi]], wrst=[pps])
            b2_score(0)
            for kb in range(nkb):
                pp, pps = (PA, PAs) if kb % 2 == 0 else (PB, PBs)
                ei = kb % 2
                if kb + 1 < nkb:
                    b2_score(kb + 1)
                act(ES[ei], pp[:, :], AF.Exp, rd=[pps, st_small], wrst=[ESst[ei]],
                    bias=NEGS[:, 0:1], scale=ISQ)
                j = kb - 4 * QB
                if j >= 0:
                    vtt(ES[ei], ES[ei], STR[:, j * 512:(j + 1) * 512], AL.mult,
                        rd=[ESst[ei], STRst], wrst=[ESst[ei]])
                mm(PR1, OCB[:, :], ES[ei], kb == 0, kb == nkb - 1,
                   rd=[ESst[ei], st_small], wrst=[PR1s])
                mm(PC[:, :], VH[:, kb * 128:(kb + 1) * 128], ES[ei],
                   kb == 0, kb == nkb - 1, rd=[VHst, ESst[ei]], wrst=[PCs])
            act(RRC[0:1, :], PR1, AF.Copy, rd=[PR1s], wrst=[RRCst])
            mm(PD[:, :], ORF[0:1, :], RRC[0:1, :], True, True, rd=[RRCst, st_small], wrst=[PDs])
            P.em("vector", (lambda o=RBB, i=PD[:, :]: nc.vector.reciprocal(o, i)),
                 reads=[PDs], writes=[RBBst])
            vtt(RES[:, h * T + QB * 512: h * T + QB * 512 + 512], PC[:, :], RBB,
                AL.mult, rd=[PCs, RBBst], wrst=[RESst[h][QB]])
    P.barrier()

    _pr2, _ep2 = make_proj_epi(xs2, xs3, None)
    fm_proj(wprojT, list(range(16)), _ep2, pre=_pr2)
    P.barrier()

    ffn_stage(xs3, out)
    P.barrier()

    semnames = sorted(P.cnt.keys())
    sems = {s: stack.enter_context(nc.semaphore(s)) for s in semnames}

    with nc.Block() as block:
        def runner(ename):
            def _run(eng):
                for waits, fn, sem, inc in P.ops[ename]:
                    for (s, v) in waits:
                        eng.wait_ge(sems[s], v)
                    if fn is not None:
                        inst = fn()
                        if inst is not None and sem is not None:
                            inst.then_inc(sems[sem], inc)
            return _run
        block.tensor(runner("tensor"))
        block.vector(runner("vector"))
        block.scalar(runner("scalar"))
        block.gpsimd(runner("gpsimd"))
        block.sync(runner("sync"))

    stack.close()
    return nc


_NC_CACHE = {}


def kernel(L0_attn_a_x, L0_attn_a_w_qkv, L0_attn_a_b_qkv, L0_attn_a_w_out,
           L0_attn_a_b_out, L0_attn_b_w_atten, L0_attn_b_w_proj,
           L0_ffn_switch_w, L0_ffn_switch_b, L0_ffn_w1, L0_ffn_b1,
           L0_ffn_w2, L0_ffn_b2, L0_ffn_w3, L0_ffn_b3):
    x = np.asarray(L0_attn_a_x, dtype=np.float32)
    wqkv = np.asarray(L0_attn_a_w_qkv, dtype=np.float32)
    bqkv = np.asarray(L0_attn_a_b_qkv, dtype=np.float32)
    wout = np.asarray(L0_attn_a_w_out, dtype=np.float32)
    bout = np.asarray(L0_attn_a_b_out, dtype=np.float32)
    watt = np.asarray(L0_attn_b_w_atten, dtype=np.float32)
    wproj = np.asarray(L0_attn_b_w_proj, dtype=np.float32)
    sw = np.asarray(L0_ffn_switch_w, dtype=np.float32)
    swb = np.asarray(L0_ffn_switch_b, dtype=np.float32)
    w1 = np.asarray(L0_ffn_w1, dtype=np.float32)
    b1 = np.asarray(L0_ffn_b1, dtype=np.float32)
    w2 = np.asarray(L0_ffn_w2, dtype=np.float32)
    b2 = np.asarray(L0_ffn_b2, dtype=np.float32)
    w3 = np.asarray(L0_ffn_w3, dtype=np.float32)
    b3 = np.asarray(L0_ffn_b3, dtype=np.float32)

    wqkvT = np.ascontiguousarray(wqkv.T)
    bqkv_cols = np.ascontiguousarray(bqkv.reshape(48, 128).T[:, 0:32])
    bout_cols = np.ascontiguousarray(bout.reshape(16, 128).T)
    b3_cols = np.ascontiguousarray(b3.reshape(16, 128).T)
    perm = np.concatenate([np.arange(0, 128, 2), np.arange(1, 128, 2)])
    wp = watt.copy()
    for h in range(H):
        wp[h * 128:(h + 1) * 128] = watt[h * 128:(h + 1) * 128][perm]
        wp[D + h * 128:D + (h + 1) * 128] = watt[D + h * 128:D + (h + 1) * 128][perm]
    watT = np.ascontiguousarray(wp.T)
    sw_route = np.ascontiguousarray(
        sw.reshape(E, ROUTE_T, KT, 128).transpose(0, 3, 2, 1).reshape(E, 128, 32768))
    w1Ta = np.ascontiguousarray(w1.transpose(0, 2, 1)).reshape(E * 128, 128)
    w2Ta = np.ascontiguousarray(w2.transpose(0, 2, 1)).reshape(E * 128, 128)
    b1Ta = np.ascontiguousarray(b1.T)
    w3Ta = np.ascontiguousarray(w3.T)

    consts = make_consts()
    common = {
        "wqkvT": wqkvT, "bqkv": np.ascontiguousarray(bqkv[2 * D:3 * D].reshape(1, D)),
        "bqkv_cols": bqkv_cols,
        "woutT": np.ascontiguousarray(wout.T), "bout_cols": bout_cols,
        "watT": watT, "wprojT": np.ascontiguousarray(wproj.T),
        "sw_route": sw_route, "swb": np.ascontiguousarray(swb.reshape(1, E)),
        "w1T": w1Ta, "w2T": w2Ta, "b1T": b1Ta, "b2m": b2,
        "w3T": w3Ta, "b3_cols": b3_cols,
        "c_id": consts["identity"], "c_ocf": consts["ones_col_f"],
        "c_ocb": consts["ones_col_b"], "c_orf": consts["ones_row_f"],
        "c_one11": consts["one11"],
        "c_dmaskT": np.ascontiguousarray(consts["dmaskT"].reshape(H * 128, 128)),
        "c_decayq": np.ascontiguousarray(consts["decayq"].reshape(H * 128, 128)),
        "c_decayk": np.ascontiguousarray(consts["decayk"].reshape(H * 128, 128)),
        "c_cos": consts["ropecos"], "c_sin": consts["ropesin"],
        "c_swp": consts["swap"],
        "c_stairs": np.ascontiguousarray(consts["stairs"].reshape(4 * 128, 512)),
        "c_negs": np.full((128, 1), EXP_SHIFT, dtype=np.float32),
    }
    in_maps = []
    for b in range(8):
        m = dict(common)
        m["xin"] = np.ascontiguousarray(x[b].T)
        in_maps.append(m)

    if "nc" not in _NC_CACHE:
        _NC_CACHE["nc"] = build_nc()
    nc = _NC_CACHE["nc"]
    res = run_bass_kernel_spmd(nc, in_maps, core_ids=list(range(8)))
    outs = [np.ascontiguousarray(np.asarray(r["out"], dtype=np.float32).T)
            for r in res.results]
    return np.stack(outs, axis=0)


if __name__ == "__main__":
    import reference as _r
    inp = _r.setup_inputs(0)
    o = kernel(**{k: np.asarray(v) for k, v in inp.items()})
    print(o.shape, o.dtype)
